# version 04 (final)
# baseline (speedup 1.0000x reference)
import contextlib
import numpy as np
import ml_dtypes
import concourse.bass as bass
import concourse.mybir as mybir
from concourse.bass_utils import run_bass_kernel_spmd

F32 = mybir.dt.float32
BF16 = mybir.dt.bfloat16
AF = mybir.ActivationFunctionType
ALU = mybir.AluOpType

D = 2048
PROJ = 8224
CONV = 6144
NH = 32
NCHUNK = 32
TC = 4096
TH = TC + 128
NSUB = TH // 128
KD = D // 128
NXS = CONV // 128
E = 8
FE = 768
FS = 1536
NFS = E * 6 + 12
T2B = 1024
N2B = TC // T2B
HBL = float(0.5 * np.log(1e5))

ALL_PHASES = ("p1a", "conv", "p1b", "carry", "p2a", "p2b")


class Tok:
    __slots__ = ("sem", "val", "eng", "kind")

    def __init__(self, sem, val, eng, kind):
        self.sem, self.val, self.eng, self.kind = sem, val, eng, kind


class KB:
    ENGS = ("tensor", "vector", "scalar", "gpsimd", "sync")

    def __init__(self, phases=ALL_PHASES, dbg=()):
        self.phases = set(phases)
        self.noalias = "noalias" in self.phases
        self.dbg = set(dbg)
        self.nc = bass.Bass("TRN2", target_bir_lowering=False, debug=False)
        self.es = contextlib.ExitStack()
        self.ops = {e: [] for e in self.ENGS}
        self.sems = {}
        for e in self.ENGS:
            self.sems[e] = self.es.enter_context(self.nc.semaphore("se_" + e))
            self.sems[e + "_q"] = self.es.enter_context(self.nc.semaphore("sq_" + e))
        self.sems["coll"] = self.es.enter_context(self.nc.semaphore("se_coll"))
        self.scnt = {k: 0 for k in self.sems}

    def emit(self, eng, fn, deps=(), give=False, dma=False, keep_self=False,
             sem_key=None):
        waits = {}
        for t in deps:
            if t is None:
                continue
            if t.kind == "eng" and t.eng == eng and not keep_self:
                continue
            waits[t.sem] = max(waits.get(t.sem, 0), t.val)
        tok = None
        if give or dma:
            key = sem_key or (eng + ("_q" if dma else ""))
            self.scnt[key] += 16 if dma else 1
            tok = Tok(key, self.scnt[key], eng, "dma" if dma else "eng")
        self.ops[eng].append((fn, tuple(waits.items()), tok))
        return tok

    def dma(self, eng, out, in_, deps=()):
        return self.emit(eng, lambda e, o=out, i=in_: e.dma_start(out=o, in_=i),
                         deps=deps, dma=True)

    def wait(self, eng, deps):
        self.emit(eng, lambda e: None, deps=deps)

    def sb(self, stack, name, shape, dt):
        return stack.enter_context(self.nc.sbuf_tensor(name, shape, dt))

    def psum(self, stack, name, shape, dt=F32):
        return stack.enter_context(self.nc.psum_tensor(name, shape, dt))

    def din(self, name, shape, dt=F32):
        return self.nc.dram_tensor(name, shape, dt, kind="ExternalInput").ap()

    def dram(self, name, shape, dt):
        if name in self.dbg:
            return self.nc.dram_tensor(name, shape, dt, kind="ExternalOutput").ap()
        return self.nc.dram_tensor(name, shape, dt).ap()

    def finalize(self):
        with self.nc.Block() as block:
            for eng_name, deco in (("tensor", block.tensor), ("vector", block.vector),
                                   ("scalar", block.scalar), ("gpsimd", block.gpsimd),
                                   ("sync", block.sync)):
                def runner(e, _n=eng_name):
                    for fn, waits, tok in self.ops[_n]:
                        for sem_key, v in waits:
                            e.wait_ge(self.sems[sem_key], v)
                        inst = fn(e)
                        if tok is not None:
                            if inst is None:
                                inst = e.nop()
                            inst.then_inc(self.sems[tok.sem],
                                          16 if tok.kind == "dma" else 1)
                deco(runner)
        self.es.close()
        return self.nc

    def build(self):
        es = self.es
        ph = self.phases

        u_main = self.din("u_main", [TC, D])
        u_halo = self.din("u_halo", [128, D])
        w_in_t = self.din("w_in_t", [D, PROJ])
        conv_w_t = self.din("conv_w_t", [128, NXS * 4])
        conv_b_t = self.din("conv_b_t", [128, NXS])
        zb_bc = self.din("zb_bc", [128, D])
        d_bc = self.din("d_bc", [128, NH])
        rmsw_t = self.din("rmsw_t", [128, KD])
        w_out_t = self.din("w_out_t", [D, D])
        router_w = self.din("router_w", [D, E])
        sh_gate = self.din("sh_gate", [D, FS])
        sh_up = self.din("sh_up", [D, FS])
        sh_down = self.din("sh_down", [FS, D])
        wg_in = self.din("wg", [E, D, FE])
        wu_in = self.din("wu", [E, D, FE])
        wd_in = self.din("wd", [E, FE, D])
        ident_c = self.din("ident_c", [128, 128])
        triu_c = self.din("triu_c", [128, 128])
        blktriu_c = self.din("blktriu_c", [128, 128])
        ones_c = self.din("ones_c", [128, 128])
        h0ones_c = self.din("h0ones_c", [128, 128])
        sel8_c = self.din("sel8_c", [8, 8 * 128])
        possel_c = self.din("possel", [128, 4])

        out = self.nc.dram_tensor("out", [TC, D], F32, kind="ExternalOutput").ap()

        sp_xbc = self.dram("sp_xbc", [NXS * 128, TC], BF16)
        sp_z = self.dram("sp_z", [TC, D], BF16)
        sp_y = self.dram("sp_y", [TC, D], F32)
        sp_x1 = self.dram("sp_x1", [TC, D], F32)
        sc_gu = self.dram("sc_gu", [2 * E + 4, D, FE], BF16)
        sc_wd = self.dram("sc_wd", [E * FE + FS, D], BF16)
        cc_in = self.nc.dram_tensor("cc_in", [128, 1040], F32)
        cc_out = self.nc.dram_tensor("cc_out", [512, 1040], F32)

        identb = self.sb(es, "identb", [128, 128], BF16)
        identf = self.sb(es, "identf", [128, 128], F32)
        one_col = self.sb(es, "one_col", [128, 1], F32)
        hbl_col = self.sb(es, "hbl_col", [128, 1], F32)
        stat = self.sb(es, "stat", [128, 8], F32)
        bnst = self.sb(es, "bnst", [128, 24], F32)
        cw_sb = self.sb(es, "cw_sb", [128, NXS * 4], F32)
        cb_sb = self.sb(es, "cb_sb", [128, NXS], F32)
        zb_sb = self.sb(es, "zb_sb", [128, D], F32)
        triub = self.sb(es, "triub", [128, 128], F32)
        blktriub = self.sb(es, "blktriub", [128, 128], F32)
        onesb = self.sb(es, "onesb", [128, 128], F32)
        h0onesb = self.sb(es, "h0onesb", [128, 128], F32)
        mask01b = self.sb(es, "mask01b", [128, 128], BF16)
        posselb = self.sb(es, "posselb", [128, 4], F32)
        es_sp = contextlib.ExitStack()
        spall = self.sb(es_sp, "spall", [128, NCHUNK * NH], F32)
        dbcb = self.sb(es_sp, "dbcb", [128, NH], F32)
        expneg_all = self.sb(es_sp, "expneg_all", [128, NCHUNK * NH], F32)
        qtile_all = self.sb(es_sp, "qtile_all", [64, NCHUNK * NH], F32)
        qtile2 = self.sb(es_sp, "qtile2", [128, NCHUNK * 16], F32)
        espn_all = self.sb(es_sp, "espn_all", [128, NCHUNK * 16], F32)
        dk_all = self.sb(es_sp, "dk_all", [128, NCHUNK * 16], F32)
        sstate = self.sb(es_sp, "sstate", [128, 1024], F32)
        nstot = self.sb(es_sp, "nstot", [128, 16], F32)
        sin_t = self.sb(es_sp, "sin_t", [128, 1024], F32)

        for sbt, drt in [(identb, ident_c), (identf, ident_c), (triub, triu_c),
                         (blktriub, blktriu_c), (onesb, ones_c), (h0onesb, h0ones_c),
                         (mask01b, triu_c), (cw_sb, conv_w_t),
                         (cb_sb, conv_b_t), (zb_sb, zb_bc),
                         (posselb, possel_c)]:
            t_cload = self.dma("gpsimd", sbt[:, :], drt)
        self.emit("vector", lambda e: e.memset(one_col[:, :], 1.0))
        t_init = self.emit("vector", lambda e: e.memset(hbl_col[:, :], HBL), give=True)

        self._cv = [None, None]

        def emit_conversion(cvb):
            jobs = []
            for e_ in range(E):
                jobs.append((wg_in[e_, :, :], sc_gu[2 * e_, :, :]))
                jobs.append((wu_in[e_, :, :], sc_gu[2 * e_ + 1, :, :]))
            jobs.append((sh_gate[:, 0:FE], sc_gu[2 * E + 0, :, :]))
            jobs.append((sh_gate[:, FE:FS], sc_gu[2 * E + 1, :, :]))
            jobs.append((sh_up[:, 0:FE], sc_gu[2 * E + 2, :, :]))
            jobs.append((sh_up[:, FE:FS], sc_gu[2 * E + 3, :, :]))
            for e_ in range(E):
                jobs.append((wd_in[e_, :, :], sc_wd[e_ * FE:(e_ + 1) * FE, :]))
            jobs.append((sh_down[0:FE, :], sc_wd[E * FE:E * FE + FE, :]))
            jobs.append((sh_down[FE:FS, :], sc_wd[E * FE + FE:E * FE + FS, :]))
            jobs2 = []
            for srcj, dstj in jobs:
                hrows = srcj.shape[0] // 2
                jobs2.append((srcj[0:hrows, :], dstj[0:hrows, :]))
                jobs2.append((srcj[hrows:, :], dstj[hrows:, :]))
            st_tok = None
            for srcj, dstj in jobs2:
                rows, cols = srcj.shape
                nk = rows // 128
                srcv = srcj.rearrange("(k p) f -> p k f", p=128)
                dstv = dstj.rearrange("(k p) f -> p k f", p=128)
                buf = cvb[:, 0:nk * cols].rearrange("p (k f) -> p k f", k=nk)
                tin = self.dma("gpsimd", buf, srcv, deps=[st_tok])
                st_tok = self.dma("sync", dstv, buf, deps=[tin])
            self._cv = [st_tok, st_tok]

        es_p1 = contextlib.ExitStack()
        ln1_fm = self.sb(es_p1, "ln1_fm", [128, KD * TH], BF16)
        es_p1p = contextlib.ExitStack()
        es_pp = contextlib.ExitStack()
        ub = self.sb(es_p1p, "ub", [128, D], F32)
        ln_tm = self.sb(es_p1p, "ln_tm", [128, D], BF16)
        p_tr1 = self.psum(es_pp, "p_tr1", [128, 2048], BF16)
        lf3 = ln1_fm[:, :].rearrange("p (k t) -> p k t", k=KD)
        t_lnfm = None
        if "p2b" in ph:
            cvb = self.sb(es_p1p, "cvb", [128, 8 * FE], BF16)
            emit_conversion(cvb)
        t_cv_done, t_cv_done2 = self._cv

        if "p1a" in ph and "noln" not in ph:
            t_ev_last = None
            t_lnu = None
            t_tp = None
            for st in range(NSUB):
                boff = 0
                src = u_halo if st == 0 else u_main[(st - 1) * 128:st * 128, :]
                t_u = self.dma("scalar", ub[:, boff:boff + D], src,
                               deps=[t_lnu])
                t_bs = None
                for j in range(4):
                    t_bs = self.emit("vector", lambda e, b=boff, j=j: e.bn_stats(
                        bnst[:, 6 * j:6 * j + 6], ub[:, b + 512 * j:b + 512 * (j + 1)]),
                        deps=[t_u] if j == 0 else (), give=(j == 3))
                t_bn = self.emit("vector", lambda e: e.bn_aggr(stat[:, 0:2], bnst[:, :]),
                                 deps=[t_bs], give=True, keep_self=True)
                self.emit("scalar", lambda e: e.activation(
                    stat[:, 2:3], stat[:, 1:2], AF.Ln, bias=one_col[:, 0:1], scale=1e5),
                    deps=[t_bn, t_init])
                t_rs = self.emit("scalar", lambda e: e.activation(
                    stat[:, 3:4], stat[:, 2:3], AF.Exp, bias=hbl_col[:, 0:1], scale=-0.5),
                    give=True)
                t_ln = self.emit("vector", lambda e, b=boff: e.tensor_scalar(
                    ln_tm[:, :], ub[:, b:b + D], stat[:, 0:1], stat[:, 3:4],
                    ALU.subtract, ALU.mult), deps=[t_rs, t_tp, t_bn], give=True,
                    keep_self=True)
                t_lnu = t_ln
                for k in range(KD):
                    t_tp = self.emit("tensor", lambda e, k=k: e.transpose(
                        p_tr1[:, k * 128:(k + 1) * 128], ln_tm[:, k * 128:(k + 1) * 128],
                        identb[:, :]),
                        deps=[t_ln, t_cload, t_ev_last] if k == 0 else (),
                        give=(k == KD - 1))
                t_ev_last = self.emit("vector", lambda e, st=st: e.tensor_copy(
                    lf3[:, :, st * 128:(st + 1) * 128],
                    p_tr1[:, :].rearrange("p (k t) -> p k t", k=KD)),
                    deps=[t_tp], give=True)
            t_lnfm = t_ev_last
            if "dbg_stat" in self.dbg:
                t = self.nc.dram_tensor("dbg_stat", [128, 8], F32,
                                        kind="ExternalOutput").ap()
                self.dma("sync", t, stat[:, :], deps=[t_lnfm])
                t2 = self.nc.dram_tensor("dbg_bnst", [128, 24], F32,
                                         kind="ExternalOutput").ap()
                self.dma("sync", t2, bnst[:, :], deps=[t_lnfm])
            if "dbg_lnfm" in self.dbg:
                t = self.nc.dram_tensor("dbg_lnfm", [128, KD * TH], BF16,
                                        kind="ExternalOutput").ap()
                self.dma("sync", t, ln1_fm[:, :], deps=[t_lnfm])

        es_pp.close()
        if not self.noalias:
            es_p1p.close()

        es_z = contextlib.ExitStack()
        es_pz = contextlib.ExitStack()
        zw = self.sb(es_z, "zw", [128, KD * 544], BF16)
        zwv = zw[:, :].rearrange("p (k c) -> p k c", k=KD)
        zbuf = self.sb(es_z, "zbuf", [128, 2 * 512], BF16)
        espb = self.sb(es_z, "espb", [128, 32], F32)
        p_z = self.psum(es_pz, "p_z", [128, 1024], F32)
        p_a = self.psum(es_pz, "p_a", [128, 1024], F32)
        t_spall = None
        t_z_spill_last = None

        if "p1a" in ph and "noz" not in ph:
            t_mm = None
            t_z1p = [None, None]
            t_aevp = [None, None]
            t_zevp = [None, None]
            for zq in range(4):
                cw0 = CONV + zq * 512
                ncol = 512 if zq < 3 else 544
                srcv = w_in_t[:, cw0:cw0 + ncol].rearrange("(k p) c -> p k c", p=128)
                if "smallzw" in ph:
                    t_zw = self.dma("gpsimd", zwv[:, 0:1, 0:ncol], srcv[:, 0:1, :],
                                    deps=[t_mm, t_lnfm, t_cv_done, t_cv_done2])
                else:
                    t_zw = self.dma("gpsimd", zwv[:, :, 0:ncol], srcv,
                                    deps=[t_mm, t_lnfm, t_cv_done, t_cv_done2])
                for m in range(NCHUNK):
                    pp = m % 2
                    pz = p_z[:, pp * 512:pp * 512 + 512]
                    pa = p_a[:, pp * 512:pp * 512 + 32]
                    for k in range(KD):
                        f = (k == 0)
                        l = (k == KD - 1)
                        dd = [t_zw, t_lnfm, t_z1p[pp], t_aevp[pp]] if f else ()
                        has_a = (ncol == 544 and "noA" not in ph)
                        t_zm = self.emit("tensor", lambda e, k=k, m=m, pz=pz, f=f, l=l:
                                         e.matmul(
                            pz[:, 0:512], lf3[:, k, (m + 1) * 128:(m + 2) * 128],
                            zwv[:, k, 0:512], start=f, stop=l), deps=dd,
                            give=(l and not has_a))
                        if has_a:
                            t_mm = self.emit("tensor", lambda e, k=k, m=m, pa=pa, f=f,
                                             l=l: e.matmul(
                                pa[:, 0:32], lf3[:, k, (m + 1) * 128:(m + 2) * 128],
                                zwv[:, k, 512:544], start=f, stop=l), give=l)
                        elif l:
                            t_mm = t_zm
                    zo = pp * 512
                    t_z1 = self.emit("vector", lambda e, pz=pz, zo=zo, zq=zq:
                                     e.tensor_tensor(
                        zbuf[:, zo:zo + 512], pz[:, :],
                        zb_sb[:, zq * 512:zq * 512 + 512], ALU.add),
                        deps=[t_mm, t_zevp[pp]], give=True)
                    t_z1p[pp] = t_z1
                    if "nospill" not in ph:
                        t_zevp[pp] = self.dma(
                            "sync", sp_z[m * 128:(m + 1) * 128, zq * 512:zq * 512 + 512],
                            zbuf[:, zo:zo + 512], deps=[t_z1])
                    if ncol == 544 and "noA" not in ph:
                        self.emit("scalar", lambda e, pa=pa: e.activation(
                            espb[:, 0:32], pa[:, :], AF.Exp), deps=[t_mm])
                        t_aevp[pp] = self.emit("scalar", lambda e, m=m: e.activation(
                            spall[:, m * 32:(m + 1) * 32], espb[:, 0:32], AF.Ln,
                            bias=one_col[:, 0:1]), give=True)
            t_spall = t_aevp[(NCHUNK - 1) % 2]
            t_z_spill_last = t_zevp[(NCHUNK - 1) % 2]
            if "dbg_sp" in self.dbg:
                t = self.nc.dram_tensor("dbg_sp", [128, NCHUNK * NH], F32,
                                        kind="ExternalOutput").ap()
                self.dma("sync", t, spall[:, :], deps=[t_spall, t_aevp[NCHUNK % 2]])

        t_zfin = [t_spall, t_z_spill_last]
        if "p1a" in ph and "noz" not in ph:
            t_zfin += [t_z1p[0], t_z1p[1], t_aevp[0], t_aevp[1], t_mm]
        es_pz.close()
        if not self.noalias:
            es_z.close()
        es_c = contextlib.ExitStack()
        wbuf = self.sb(es_c, "wbuf", [128, 2 * KD * 128], BF16)
        wbv = wbuf[:, :].rearrange("p (h k c) -> p h k c", h=2, k=KD)
        ext = self.sb(es_c, "ext", [128, TH + 3], BF16)
        acc_a = self.sb(es_c, "acc_a", [128, TH], BF16)
        es_pc = contextlib.ExitStack()
        p_xbc = self.psum(es_pc, "p_xbc", [128, 2560], F32)
        t_conv_done = None

        if "conv" in ph:
            t_w_used = [None, None]
            t_evac_last = None
            t_si = None
            t_sx_used = None
            for os_ in range(NXS):
                hf = os_ % 2
                srcv = w_in_t[:, os_ * 128:(os_ + 1) * 128].rearrange(
                    "(k p) c -> p k c", p=128)
                t_w = self.dma("gpsimd", wbv[:, hf, :, :], srcv,
                               deps=[t_w_used[hf]] + (t_zfin if os_ < 2 else []))
                t_half = [None, None]
                nck_ = (TH + 511) // 512
                h1_ = max(1, nck_ // 2)
                halves_ = [(0, h1_), (h1_, nck_ - h1_)] if nck_ > h1_ else [(0, h1_)]
                for hfx, (c0, nch) in enumerate(halves_):
                    t_mmx = None
                    for k in range(KD):
                        f = (k == 0)
                        l = (k == KD - 1)
                        for ci in range(nch):
                            tcw = min(512, TH - (c0 + ci) * 512)
                            dd = ([t_w, t_lnfm, t_evac_last] + (t_zfin if os_ == 0 else [])) if (f and ci == 0) else ()
                            t_mmx = self.emit("tensor", lambda e, k=k, c=c0 + ci, ci=ci,
                                              tcw=tcw, hf=hf, f=f, l=l: e.matmul(
                                p_xbc[:, ci * 512:ci * 512 + tcw], wbv[:, hf, k, :],
                                lf3[:, k, c * 512:c * 512 + tcw], start=f, stop=l),
                                deps=dd, give=(l and ci == nch - 1))
                    t_half[hfx] = self.emit("vector", lambda e, c0=c0, nch=nch:
                                            e.tensor_copy(
                        ext[:, 3 + c0 * 512: 3 + min((c0 + nch) * 512, TH)],
                        p_xbc[:, 0:min(nch * 512, TH - c0 * 512)]),
                        deps=[t_mmx, t_si], give=True)
                    t_evac_last = t_half[hfx]
                t_w_used[hf] = t_mmx
                self.emit("vector", lambda e: e.memset(ext[:, 0:3], 0.0))
                cwc = cw_sb[:, os_ * 4:os_ * 4 + 4]
                self.emit("vector", lambda e, c=cwc: e.tensor_scalar(
                    acc_a[:, :], ext[:, 0:TH], c[:, 0:1], None, ALU.mult),
                    deps=[t_cload, t_sx_used, t_si])
                t_cv2 = None
                for w in (1, 2, 3):
                    t_cv2 = self.emit("vector", lambda e, c=cwc, w=w:
                                      e.scalar_tensor_tensor(
                        acc_a[:, :], ext[:, w:w + TH], c[:, w:w + 1], acc_a[:, :],
                        ALU.mult, ALU.add), give=(w == 3))
                t_si = self.emit("scalar", lambda e, os_=os_: e.activation(
                    acc_a[:, :], acc_a[:, :], AF.Silu, bias=cb_sb[:, os_:os_ + 1]),
                    deps=[t_cv2], give=True)
                t_sx_used = self.dma("sync", sp_xbc[os_ * 128:(os_ + 1) * 128, :],
                                     acc_a[:, 128:TH], deps=[t_si])
                if os_ == 0 and "dbg_acc" in self.dbg:
                    ta = self.nc.dram_tensor("dbg_acc", [128, TH], BF16,
                                             kind="ExternalOutput").ap()
                    self.dma("sync", ta, acc_a[:, :], deps=[t_si])
                    tb = self.nc.dram_tensor("dbg_ext", [128, TH + 3], BF16,
                                             kind="ExternalOutput").ap()
                    self.dma("sync", tb, ext[:, :], deps=[t_si])
            t_conv_done = t_sx_used
        es_pc.close()
        if not self.noalias:
            es_c.close()
        if not self.noalias:
            es_p1.close()

        t_dbc = self.dma("gpsimd", dbcb[:, :], d_bc, deps=[t_conv_done])
        self.emit("vector", lambda e: e.memset(nstot[:, :], 0.0), deps=[t_conv_done])
        self.emit("vector", lambda e: e.memset(sstate[:, :], 0.0))
        t_init2 = self.emit("vector", lambda e: e.memset(dk_all[:, 0:16], 1.0),
                            give=True)

        es_b = contextlib.ExitStack()
        xcb = self.sb(es_b, "xcb", [128, 2 * NXS * 128], BF16)
        xtm = self.sb(es_b, "xtm", [128, D], BF16)
        btm = self.sb(es_b, "btm", [128, D], BF16)
        xs_t = self.sb(es_b, "xs_t", [128, D], BF16)
        xd_t = self.sb(es_b, "xd_t", [128, D], BF16)
        bd_t = self.sb(es_b, "bd_t", [128, D], BF16)
        p4b = self.sb(es_b, "p4b", [128, 2 * 512], BF16)
        sprevb = self.sb(es_b, "sprevb", [128, 1024], BF16)
        sprevqb = self.sb(es_b, "sprevqb", [128, 1024], BF16)
        ybuf = self.sb(es_b, "ybuf", [128, 2 * D], F32)
        expnswb = self.sb(es_b, "expnswb", [128, NH], F32)
        dst_t = self.sb(es_b, "dst_t", [128, NH], F32)
        ns_sb = self.sb(es_b, "ns_sb", [128, NH], F32)
        sscr = self.sb(es_b, "sscr", [128, 128], F32)
        es_pb = contextlib.ExitStack()
        p_trs = self.psum(es_pb, "p_trs", [128, 2048], BF16)
        p_st = self.psum(es_pb, "p_st", [128, 512], F32)
        p_gt = [self.psum(es_pb, f"p_gt{i}", [128, 512], F32) for i in range(2)]
        p_y = [self.psum(es_pb, f"p_y{i}", [128, 512], F32) for i in range(2)]
        p_s = self.psum(es_pb, "p_s", [128, 512], F32)

        t_p1b_done = None
        t_nst = None
        t_y_spill = [None, None]
        if "p1b" in ph:
            t_xcb_used = [None, None]
            t_ymm = None
            t_pst_users = [t_spall]
            t_trs_used = None
            t_xs = None
            t_bd = None
            t_dst = None
            t_gt_ev = [None, None]
            t_y_ev = [None, None]
            t_s_ev = None
            t_scan = None
            t_sprev_user = None
            for c in range(NCHUNK):
                xo = (c % 2) * NXS * 128
                xv = xcb[:, xo:xo + NXS * 128].rearrange("p (s t) -> p s t", s=NXS)
                serd = ([t_scan, t_y_ev[0], t_y_ev[1], t_p4o, t_spq, t_bd, t_xs]
                        if c > 0 else [])
                t_x = None
                for g3 in range(3):
                    t_x = self.dma("scalar", xv[:, g3 * 16:(g3 + 1) * 16, :],
                                   sp_xbc[g3 * 2048:(g3 + 1) * 2048,
                                          c * 128:(c + 1) * 128]
                                   .rearrange("(s p) t -> p s t", p=128),
                                   deps=[t_conv_done, t_xcb_used[c % 2]] + serd)
                spc = spall[:, c * NH:(c + 1) * NH]
                self.emit("tensor", lambda e, spc=spc: e.matmul(
                    p_st[:, 0:32], triub[:, :], spc, start=True, stop=True),
                    deps=t_pst_users + [t_spall, t_conv_done] + serd)
                self.emit("tensor", lambda e, spc=spc: e.matmul(
                    p_st[:, 32:64], blktriub[:, :], spc, start=True, stop=True))
                self.emit("tensor", lambda e, spc=spc: e.matmul(
                    p_st[:, 64:96], onesb[:, :], spc, start=True, stop=True))
                t_stm = self.emit("tensor", lambda e, spc=spc: e.matmul(
                    p_st[:, 96:128], h0onesb[:, :], spc, start=True, stop=True),
                    give=True)
                self.emit("scalar", lambda e: e.activation(
                    expnswb[:, :], p_st[:, 32:64], AF.Exp), deps=[t_stm, t_xs])
                self.emit("scalar", lambda e, c=c: e.activation(
                    expneg_all[:, c * NH:(c + 1) * NH], p_st[:, 32:64], AF.Exp,
                    scale=-1.0))
                self.emit("scalar", lambda e, c=c: e.activation(
                    qtile_all[0:64, c * NH:(c + 1) * NH], p_st[0:64, 96:128],
                    AF.Exp, scale=-1.0))
                self.emit("scalar", lambda e, c=c: e.activation(
                    qtile2[0:64, c * 16:(c + 1) * 16],
                    p_st[0:64, 96:128].rearrange("p (a b) -> p a b", b=2)[:, :, 0:1],
                    AF.Exp, scale=-1.0))
                self.emit("scalar", lambda e, c=c: e.activation(
                    qtile2[64:128, c * 16:(c + 1) * 16],
                    p_st[64:128, 96:128].rearrange("p (a b) -> p a b", b=2)[:, :, 1:2],
                    AF.Exp, scale=-1.0))
                self.emit("scalar", lambda e, c=c: e.activation(
                    espn_all[0:64, c * 16:(c + 1) * 16],
                    p_st[0:64, 64:96].rearrange("p (a b) -> p a b", b=2)[:, :, 0:1],
                    AF.Exp, scale=-1.0))
                t_esp = self.emit("scalar", lambda e, c=c: e.activation(
                    espn_all[64:128, c * 16:(c + 1) * 16],
                    p_st[64:128, 64:96].rearrange("p (a b) -> p a b", b=2)[:, :, 1:2],
                    AF.Exp, scale=-1.0), give=True)
                self.emit("vector", lambda e: e.tensor_copy(ns_sb[:, :], p_st[:, 0:32]),
                          deps=[t_stm, t_dst])
                t_nsv = self.emit("vector", lambda e: e.tensor_tensor(
                    ns_sb[:, :], ns_sb[:, :], p_st[:, 64:96], ALU.subtract), give=True)
                t_dst = self.emit("scalar", lambda e: e.activation(
                    dst_t[:, :], ns_sb[:, :], AF.Exp), deps=[t_nsv, t_bd], give=True)
                self.emit("vector", lambda e: e.tensor_tensor(
                    nstot[0:64, :], nstot[0:64, :],
                    p_st[0:64, 64:96].rearrange("p (a b) -> p a b", b=2)[:, :, 0:1]
                    .squeeze(2), ALU.add))
                t_nst = self.emit("vector", lambda e: e.tensor_tensor(
                    nstot[64:128, :], nstot[64:128, :],
                    p_st[64:128, 64:96].rearrange("p (a b) -> p a b", b=2)[:, :, 1:2]
                    .squeeze(2), ALU.add), give=True)
                t_pst_users = [t_esp, t_dst, t_nst, t_nsv]
                if c + 1 < NCHUNK:
                    self.emit("vector", lambda e, c=c: e.tensor_tensor(
                        dk_all[:, (c + 1) * 16:(c + 2) * 16],
                        dk_all[:, c * 16:(c + 1) * 16],
                        espn_all[:, c * 16:(c + 1) * 16], ALU.mult),
                        deps=[t_esp, t_init2])
                t_tx = None
                for k in range(KD):
                    t_tx = self.emit("tensor", lambda e, k=k, xv=xv: e.transpose(
                        p_trs[:, k * 128:(k + 1) * 128], xv[:, k, :], identb[:, :]),
                        deps=[t_x, t_trs_used] if k == 0 else (), give=(k == KD - 1))
                t_xev = self.emit("vector", lambda e: e.tensor_copy(xtm[:, :], p_trs[:, :]),
                                  deps=[t_tx], give=True)
                t_tb = None
                for k in range(KD):
                    t_tb = self.emit("tensor", lambda e, k=k, xv=xv: e.transpose(
                        p_trs[:, k * 128:(k + 1) * 128], xv[:, 16 + k, :], identb[:, :]),
                        deps=[t_xev] if k == 0 else (), give=(k == KD - 1))
                t_bev = self.emit("vector", lambda e: e.tensor_copy(btm[:, :], p_trs[:, :]),
                                  deps=[t_tb], give=True)
                t_trs_used = t_bev
                env = expnswb[:, :].unsqueeze(2).to_broadcast([128, NH, 64])
                self.emit("vector", lambda e, env=env: e.tensor_tensor(
                    xs_t[:, :].rearrange("p (h q) -> p h q", h=NH),
                    xtm[:, :].rearrange("p (h q) -> p h q", h=NH), env, ALU.mult),
                    deps=[t_xev])
                dbv = dbcb[:, :].unsqueeze(2).to_broadcast([128, NH, 64])
                t_xs = self.emit("vector", lambda e, dbv=dbv: e.tensor_tensor(
                    xd_t[:, :].rearrange("p (h q) -> p h q", h=NH),
                    xs_t[:, :].rearrange("p (h q) -> p h q", h=NH), dbv, ALU.mult),
                    deps=[t_dbc], give=True)
                dsv = dst_t[:, :].unsqueeze(2).to_broadcast([128, NH, 64])
                t_bd = self.emit("vector", lambda e, dsv=dsv: e.tensor_tensor(
                    bd_t[:, :].rearrange("p (h q) -> p h q", h=NH),
                    btm[:, :].rearrange("p (h q) -> p h q", h=NH), dsv, ALU.mult),
                    deps=[t_dst, t_bev], give=True)
                self.emit("vector", lambda e: e.tensor_copy(sprevb[:, :], sstate[:, :]),
                          deps=[t_sprev_user])
                qv = qtile2[:, c * 16:(c + 1) * 16].unsqueeze(2)\
                    .to_broadcast([128, 16, 64])
                t_spq = self.emit("vector", lambda e, qv=qv: e.tensor_tensor(
                    sprevqb[:, :].rearrange("p (h q) -> p h q", h=16),
                    sprevb[:, :].rearrange("p (h q) -> p h q", h=16), qv, ALU.mult),
                    deps=[t_esp], give=True)
                ptf = p_trs[:, :].bitcast(F32)
                for g in range(8):
                    pg = p_gt[g % 2]
                    py = p_y[g % 2]
                    pgo = ptf[:, (g % 2) * 512:(g % 2) * 512 + 256]
                    hs0 = g * 4
                    t_gtm = None
                    for hh in range(4):
                        h = hs0 + hh
                        pb = (h % 2) * 64
                        tgt = (pg[:, (hh // 2) * 128:(hh // 2) * 128 + 128]
                               if h % 2 == 0 else
                               pgo[:, (hh // 2) * 128:(hh // 2) * 128 + 128])
                        t_gtm = self.emit("tensor", lambda e, tgt=tgt, h=h, pb=pb,
                                          xv=xv: e.matmul(
                            tgt, xv[pb:pb + 64, 16 + h // 2, :],
                            xv[pb:pb + 64, 32 + h // 2, :], start=True, stop=True),
                            deps=[t_gt_ev[g % 2], t_x, t_bev] if hh == 0 else (),
                            give=(hh == 3))
                    po = (g % 2) * 512
                    mkv2 = mask01b[:, :].unsqueeze(1).to_broadcast([128, 2, 128])
                    p4v = p4b[:, po:po + 512].rearrange("p (a b t) -> p a b t",
                                                        a=2, b=2)
                    self.emit("vector", lambda e, pg=pg, p4v=p4v, mkv2=mkv2:
                              e.tensor_tensor(
                        p4v[:, :, 0, :],
                        pg[:, 0:256].rearrange("p (gg t) -> p gg t", gg=2), mkv2,
                        ALU.mult), deps=[t_gtm, t_y_ev[g % 2]])
                    self.emit("vector", lambda e, pgo=pgo, p4v=p4v, mkv2=mkv2:
                              e.tensor_tensor(
                        p4v[:, :, 1, :],
                        pgo[:, 0:256].rearrange("p (gg t) -> p gg t", gg=2), mkv2,
                        ALU.mult))
                    qv4 = qtile_all[0:64, c * NH + hs0:c * NH + hs0 + 4].unsqueeze(2)\
                        .to_broadcast([64, 4, 64])
                    t_p4o = self.emit("vector", lambda e, po=po, qv4=qv4:
                                      e.tensor_tensor(
                        p4b[0:64, po:po + 512].rearrange("p (gg t) -> p gg t", gg=4)
                        [:, :, 64:128],
                        p4b[0:64, po:po + 512].rearrange("p (gg t) -> p gg t", gg=4)
                        [:, :, 64:128], qv4, ALU.mult),
                        deps=[t_esp], give=True)
                    t_ymm = None
                    for hh in range(4):
                        h = hs0 + hh
                        self.emit("tensor", lambda e, hh=hh, h=h, py=py, po=po: e.matmul(
                            py[:, hh * 64:(hh + 1) * 64],
                            p4b[:, po + hh * 128:po + (hh + 1) * 128],
                            xs_t[:, h * 64:(h + 1) * 64], start=(hh == 0), stop=False),
                            deps=[t_p4o, t_xs, t_spq, t_y_ev[g % 2]] if hh == 0 else ())
                    for hh in range(4):
                        h = hs0 + hh
                        pb = (h % 2) * 64
                        self.emit("tensor", lambda e, hh=hh, h=h, py=py, pb=pb,
                                  xv=xv: e.matmul(
                            py[0:64, hh * 64:(hh + 1) * 64],
                            xv[pb:pb + 64, 32 + h // 2, 0:64],
                            sprevb[pb:pb + 64, (h // 2) * 64:(h // 2 + 1) * 64],
                            start=False, stop=False, tile_position=(pb, 0)))
                        self.emit("tensor", lambda e, hh=hh, h=h, py=py, pb=pb,
                                  xv=xv: e.matmul(
                            py[64:128, hh * 64:(hh + 1) * 64],
                            xv[pb:pb + 64, 32 + h // 2, 64:128],
                            sprevqb[pb:pb + 64, (h // 2) * 64:(h // 2 + 1) * 64],
                            start=False, stop=False, tile_position=(pb, 64)))
                        t_ymm = self.emit("tensor", lambda e, hh=hh, h=h, py=py: e.matmul(
                            py[:, hh * 64:(hh + 1) * 64], identb[:, :],
                            xd_t[:, h * 64:(h + 1) * 64], start=False, stop=(hh == 3)),
                            give=(hh == 3))
                    t_smm = None
                    for hh in range(4):
                        h = hs0 + hh
                        t_smm = self.emit("tensor", lambda e, hh=hh, h=h: e.matmul(
                            p_s[0:64, hh * 64:(hh + 1) * 64],
                            bd_t[:, h * 64:(h + 1) * 64],
                            xtm[:, h * 64:(h + 1) * 64], start=True, stop=True),
                            deps=[t_s_ev, t_bd] if hh == 0 else (), give=(hh == 3))
                    engv = expneg_all[:, c * NH + hs0:c * NH + hs0 + 4].unsqueeze(2)\
                        .to_broadcast([128, 4, 64])
                    yo = (c % 2) * D
                    t_y_ev[g % 2] = self.emit("vector", lambda e, py=py, g=g, yo=yo,
                                              engv=engv: e.tensor_tensor(
                        ybuf[:, yo + g * 256: yo + (g + 1) * 256].rearrange(
                            "p (gg q) -> p gg q", gg=4),
                        py[:, 0:256].rearrange("p (gg q) -> p gg q", gg=4), engv,
                        ALU.mult),
                        deps=[t_ymm, t_y_spill[c % 2]], give=True)
                    t_gt_ev[g % 2] = t_y_ev[g % 2]
                    for pb2 in (0, 64):
                        slc = sstate[pb2:pb2 + 64, g * 128:(g + 1) * 128]
                        epv = espn_all[pb2:pb2 + 64, c * 16 + 2 * g:c * 16 + 2 * g + 2]\
                            .unsqueeze(2).to_broadcast([64, 2, 64])
                        psv = p_s[0:64, 0:256].rearrange(
                            "p (a b q) -> p a b q", a=2, b=2)[:, :, pb2 // 64, :]
                        self.emit("vector", lambda e, slc=slc, epv=epv: e.tensor_tensor(
                            sscr[0:64, :].rearrange("p (gg q) -> p gg q", gg=2),
                            slc.rearrange("p (gg q) -> p gg q", gg=2), epv, ALU.mult),
                            deps=[t_smm, t_esp])
                        t_scan = self.emit("vector", lambda e, slc=slc, psv=psv:
                                           e.tensor_tensor(
                            slc.rearrange("p (gg q) -> p gg q", gg=2),
                            sscr[0:64, :].rearrange("p (gg q) -> p gg q", gg=2),
                            psv, ALU.add), give=True)
                    t_s_ev = t_scan
                t_sprev_user = t_scan
                t_trs_used = t_p4o
                t_xcb_used[c % 2] = t_ymm if t_ymm is not None else t_bd
                yo = (c % 2) * D
                t_y_spill[c % 2] = self.dma(
                    "sync", sp_y[c * 128:(c + 1) * 128, :], ybuf[:, yo:yo + D],
                    deps=[t_y_ev[0], t_y_ev[1]])
            t_p1b_done = t_scan
            if "dbg_state" in self.dbg:
                t = self.nc.dram_tensor("dbg_state", [128, 1024], F32,
                                        kind="ExternalOutput").ap()
                self.dma("sync", t, sstate[:, :], deps=[t_p1b_done])
            if "dbg_meta" in self.dbg:
                for nm, tile, hh in [("dbg_dk", dk_all, 64), ("dbg_q", qtile_all, 64),
                                     ("dbg_en", expneg_all, 128),
                                     ("dbg_es", espn_all, 64)]:
                    t = self.nc.dram_tensor(nm, [hh, NCHUNK * NH], F32,
                                            kind="ExternalOutput").ap()
                    self.dma("sync", t, tile[0:hh, :], deps=[t_p1b_done])

        if "carry" in ph:
            es_g = contextlib.ExitStack()
            gath = self.sb(es_g, "gath", [128, 4 * 1040], F32)
            spre = self.sb(es_g, "spre", [128, 1024], F32)
            sscr2 = self.sb(es_g, "sscr2", [128, 1024], F32)
            dspn = self.sb(es_g, "dspn", [128, 16], F32)
            t_dsp = self.emit("scalar", lambda e: e.activation(
                dspn[:, :], nstot[:, :], AF.Exp, scale=-1.0),
                deps=[t_p1b_done, t_nst], give=True)
            t_c1 = self.dma("sync", cc_in.ap()[:, 0:1024], sstate[:, :],
                            deps=[t_p1b_done])
            t_c2 = self.dma("sync", cc_in.ap()[:, 1024:1040], dspn[:, :],
                            deps=[t_dsp])
            t_cc = self.emit("gpsimd", lambda e: e.collective_compute(
                "AllGather", ALU.bypass,
                replica_groups=[[0, 1, 2, 3], [4, 5, 6, 7]],
                ins=[cc_in.ap().opt()], outs=[cc_out.ap().opt()]),
                deps=[t_c1, t_c2], give=True, sem_key="coll")
            t_g = self.dma("sync",
                           gath[:, :].rearrange("p (r c) -> p r c", r=4),
                           cc_out.ap().rearrange("(r p) c -> p r c", p=128),
                           deps=[t_cc])
            self.emit("vector", lambda e: e.memset(spre[:, :], 0.0), deps=[t_g])
            self.emit("vector", lambda e: e.memset(sin_t[:, :], 0.0))
            for j in range(4):
                gw = j * 1040
                t_sin = self.emit("vector", lambda e, j=j: e.scalar_tensor_tensor(
                    sin_t[:, :], spre[:, :], posselb[:, j:j + 1], sin_t[:, :],
                    ALU.mult, ALU.add), deps=[t_cload], give=(j == 3))
                if j < 3:
                    dspv = gath[:, gw + 1024:gw + 1040].unsqueeze(2)\
                        .to_broadcast([128, 16, 64])
                    self.emit("vector", lambda e, dspv=dspv: e.tensor_tensor(
                        sscr2[:, :].rearrange("p (h q) -> p h q", h=16),
                        spre[:, :].rearrange("p (h q) -> p h q", h=16), dspv, ALU.mult))
                    self.emit("vector", lambda e, gw=gw: e.tensor_tensor(
                        spre[:, :], sscr2[:, :], gath[:, gw:gw + 1024], ALU.add))
            if not self.noalias:
                es_g.close()
            if "dbg_sin" in self.dbg:
                t = self.nc.dram_tensor("dbg_sin", [128, 1024], F32,
                                        kind="ExternalOutput").ap()
                self.dma("sync", t, sin_t[:, :], deps=[t_sin])
        else:
            t_sin = self.emit("vector", lambda e: e.memset(sin_t[:, :], 0.0), give=True)
        es_pb.close()
        if not self.noalias:
            es_b.close()

        if "p2a" not in ph and "p2b" not in ph:
            for qk in ("sync_q", "scalar_q", "gpsimd_q"):
                if self.scnt[qk]:
                    self.wait("sync", [Tok(qk, self.scnt[qk], "x", "dma")])
            for st_ in (es_b, es_c, es_z, es_p1p, es_p1, es_sp):
                st_.close()
            return self.finalize()

        es_a = contextlib.ExitStack()
        wout_sb = self.sb(es_a, "wout_sb", [128, KD * D], BF16)
        wov = wout_sb[:, :].rearrange("p (k d) -> p k d", k=KD)
        rmsw_sb = self.sb(es_a, "rmsw_sb", [128, KD], F32)
        ycb = self.sb(es_a, "ycb", [128, 2 * D], F32)
        zcb = self.sb(es_a, "zcb", [128, 2 * D], BF16)
        ucb = self.sb(es_a, "ucb", [128, 2 * D], F32)
        ccb = self.sb(es_a, "ccb", [128, 2 * 16 * 128], BF16)
        sdb = self.sb(es_a, "sdb", [128, 1024], BF16)
        sdqb = self.sb(es_a, "sdqb", [128, 1024], BF16)
        zsil = self.sb(es_a, "zsil", [128, D], BF16)
        ygf = self.sb(es_a, "ygf", [128, D], F32)
        ygb = self.sb(es_a, "ygb", [128, D], BF16)
        crscr = self.sb(es_a, "crscr", [128, 512], F32)
        sqscr = self.sb(es_a, "sqscr", [128, D], BF16)
        ynfm = self.sb(es_a, "ynfm", [128, D], BF16)
        x1b = self.sb(es_a, "x1b", [128, 2 * D], F32)
        es_pa = contextlib.ExitStack()
        p_cr = self.psum(es_pa, "p_cr", [128, 1024], F32)
        p_tr2 = self.psum(es_pa, "p_tr2", [128, 2048], BF16)
        p_o = self.psum(es_pa, "p_o", [128, 2048], F32)

        t_p2a_done = None
        if "p2a" in ph:
            t_wo = self.dma("gpsimd", wov[:, :, :],
                            w_out_t.rearrange("(k p) d -> p k d", p=128),
                            deps=[t_conv_done, t_sin])
            t_rw = self.dma("scalar", rmsw_sb[:, :], rmsw_t, deps=[t_sin])
            t_fold = None
            for k in range(KD):
                t_fold = self.emit("vector", lambda e, k=k: e.tensor_scalar(
                    wov[:, k, :], wov[:, k, :], rmsw_sb[:, k:k + 1], None, ALU.mult),
                    deps=[t_wo, t_rw] if k == 0 else (), give=(k == KD - 1))
            t_ld_used = [None, None]
            t_x1_dma = [None, None]
            t_sq = None
            t_yg = None
            t_ygb = None
            t_om = None
            t_zs = None
            t_dbg_ygf = None
            for c in range(NCHUNK):
                pp = c % 2
                r0 = c * 128
                t_yl = self.dma("sync", ycb[:, pp * D:(pp + 1) * D],
                                sp_y[r0:r0 + 128, :],
                                deps=[t_ld_used[pp], t_y_spill[(NCHUNK - 1) % 2],
                                      t_sin])
                t_zl = self.dma("sync", zcb[:, pp * D:(pp + 1) * D],
                                sp_z[r0:r0 + 128, :],
                                deps=[t_z_spill_last, t_zs])
                t_ul = self.dma("scalar", ucb[:, pp * D:(pp + 1) * D],
                                u_main[r0:r0 + 128, :],
                                deps=[t_ld_used[pp], t_sin])
                ccv = ccb[:, pp * 16 * 128:(pp + 1) * 16 * 128].rearrange(
                    "p (s t) -> p s t", s=16)
                t_cl = self.dma("sync", ccv,
                                sp_xbc[32 * 128:48 * 128, r0:r0 + 128]
                                .rearrange("(s p) t -> p s t", p=128),
                                deps=[t_ld_used[pp], t_conv_done])
                dkv = dk_all[:, c * 16:(c + 1) * 16].unsqueeze(2)\
                    .to_broadcast([128, 16, 64])
                t_sd = self.emit("vector", lambda e, dkv=dkv: e.tensor_tensor(
                    sdb[:, :].rearrange("p (h q) -> p h q", h=16),
                    sin_t[:, :].rearrange("p (h q) -> p h q", h=16), dkv, ALU.mult),
                    deps=[t_sin], give=True)
                qv = qtile2[:, c * 16:(c + 1) * 16].unsqueeze(2)\
                    .to_broadcast([128, 16, 64])
                t_sdq = self.emit("vector", lambda e, qv=qv: e.tensor_tensor(
                    sdqb[:, :].rearrange("p (h q) -> p h q", h=16),
                    sdb[:, :].rearrange("p (h q) -> p h q", h=16), qv, ALU.mult),
                    give=True)
                t_zs = self.emit("scalar", lambda e, pp=pp: e.activation(
                    zsil[:, :], zcb[:, pp * D:(pp + 1) * D], AF.Silu),
                    deps=[t_zl, t_yg], give=True)
                if "dbg_p2a0" in self.dbg and c == 0:
                    dbg_zsil0 = self.nc.dram_tensor(
                        "dbg_zsil0", [128, D], BF16, kind="ExternalOutput").ap()
                    dbg_ycb0 = self.nc.dram_tensor(
                        "dbg_ycb0", [128, D], F32, kind="ExternalOutput").ap()
                    self.dma("sync", dbg_zsil0[:, :], zsil[:, :], deps=[t_zs])
                    self.dma("sync", dbg_ycb0[:, :], ycb[:, 0:D], deps=[t_yl])
                t_cr_ev = None
                t_crm_last = None
                for g in range(4):
                    hs0 = g * 8
                    t_crm = None
                    for hh in range(8):
                        h = hs0 + hh
                        pb = (h % 2) * 64
                        cro = (h % 2) * 512 + (hh // 2) * 64
                        self.emit("tensor", lambda e, hh=hh, h=h, pb=pb, cro=cro,
                                  ccv=ccv: e.matmul(
                            p_cr[0:64, cro:cro + 64],
                            ccv[pb:pb + 64, h // 2, 0:64],
                            sdb[pb:pb + 64, (h // 2) * 64:(h // 2 + 1) * 64],
                            start=True, stop=True, tile_position=(pb, 0)),
                            deps=[t_cl, t_sd, t_sdq, t_cr_ev] if hh == 0 else ())
                        t_crm = self.emit("tensor", lambda e, hh=hh, h=h, pb=pb,
                                          cro=cro, ccv=ccv: e.matmul(
                            p_cr[64:128, cro:cro + 64],
                            ccv[pb:pb + 64, h // 2, 64:128],
                            sdqb[pb:pb + 64, (h // 2) * 64:(h // 2 + 1) * 64],
                            start=True, stop=True, tile_position=(pb, 64)))
                    t_crm = self.emit("tensor", lambda e: e.matmul(
                        p_cr[0:1, 256:257], identb[:, 0:1], identb[:, 0:1],
                        start=True, stop=True), give=True)
                    env4 = expneg_all[:, c * NH + hs0:c * NH + hs0 + 8]\
                        .rearrange("p (a b) -> p a b", b=2)
                    crv = crscr[:, :].rearrange("p (gg par q) -> p gg par q", gg=4,
                                                par=2)
                    pcv = p_cr[:, :].rearrange("p (par u gg q) -> p par u gg q",
                                               par=2, u=2, gg=4)
                    self.emit("vector", lambda e, crv=crv, pcv=pcv, env4=env4:
                              e.tensor_tensor(
                        crv[:, :, 0, :], pcv[:, 0, 0, :, :],
                        env4[:, :, 0:1].to_broadcast([128, 4, 64]), ALU.mult),
                        deps=[t_crm, t_ygb])
                    t_ce = self.emit("vector", lambda e, crv=crv, pcv=pcv, env4=env4:
                                     e.tensor_tensor(
                        crv[:, :, 1, :], pcv[:, 1, 0, :, :],
                        env4[:, :, 1:2].to_broadcast([128, 4, 64]), ALU.mult),
                        give=True)
                    t_cr_ev = self.emit("vector", lambda e, g=g, pp=pp: e.tensor_tensor(
                        ygf[:, g * 512:(g + 1) * 512],
                        ycb[:, pp * D + g * 512:pp * D + (g + 1) * 512],
                        crscr[:, :], ALU.add),
                        deps=[t_yl, t_sq, t_dbg_ygf], give=True)
                    t_crm_last = t_crm
                t_yg = self.emit("vector", lambda e: e.tensor_tensor(
                    ygf[:, :], ygf[:, :], zsil[:, :], ALU.mult),
                    deps=[t_zs, t_cr_ev], give=True)
                if "dbg_ygf" in self.dbg:
                    if c == 0:
                        dbg_ygf = self.nc.dram_tensor(
                            "dbg_ygf", [TC, D], F32, kind="ExternalOutput").ap()
                    t_dbg_ygf = self.dma("sync", dbg_ygf[r0:r0 + 128, :],
                                         ygf[:, :], deps=[t_yg])
                t_sq = self.emit("scalar", lambda e: e.activation(
                    sqscr[:, :], ygf[:, :], AF.Square, accum_out=stat[:, 5:6]),
                    deps=[t_yg], give=True)
                self.emit("scalar", lambda e: e.activation(
                    stat[:, 6:7], stat[:, 5:6], AF.Ln, bias=one_col[:, 0:1],
                    scale=1e5 / D))
                t_rst = self.emit("scalar", lambda e: e.activation(
                    stat[:, 7:8], stat[:, 6:7], AF.Exp, bias=hbl_col[:, 0:1],
                    scale=-0.5), give=True)
                if "dbg_rstd" in self.dbg:
                    if c == 0:
                        dbg_rstd = self.nc.dram_tensor(
                            "dbg_rstd", [128, NCHUNK * 3], F32,
                            kind="ExternalOutput").ap()
                    self.dma("sync", dbg_rstd[:, c * 3:c * 3 + 3],
                             stat[:, 5:8], deps=[t_rst])
                t_ygb = self.emit("vector", lambda e: e.tensor_copy(ygb[:, :], ygf[:, :]),
                                  deps=[t_yg, t_om], give=True)
                t_tp = None
                for k in range(KD):
                    t_tp = self.emit("tensor", lambda e, k=k: e.transpose(
                        p_tr2[:, k * 128:(k + 1) * 128], ygb[:, k * 128:(k + 1) * 128],
                        identb[:, :]), deps=[t_ygb] if k == 0 else (),
                        give=(k == KD - 1))
                t_yn = self.emit("vector", lambda e: e.tensor_copy(
                    ynfm[:, :], p_tr2[:, :]), deps=[t_tp], give=True)
                t_omp = None
                for k in range(KD):
                    f = (k == 0)
                    l = (k == KD - 1)
                    for dc in range(4):
                        t_omp = self.emit("tensor", lambda e, k=k, dc=dc, f=f, l=l:
                                          e.matmul(
                            p_o[:, dc * 512:(dc + 1) * 512],
                            ynfm[:, k * 128:(k + 1) * 128],
                            wov[:, k, dc * 512:(dc + 1) * 512], start=f, stop=l),
                            deps=[t_yn, t_fold, t_ld_used[pp], t_ld_used[1 - pp]]
                            if (f and dc == 0) else (), give=(l and dc == 3))
                t_om = t_omp
                t_x1 = self.emit("vector", lambda e, pp=pp: e.scalar_tensor_tensor(
                    x1b[:, pp * D:(pp + 1) * D], p_o[:, :], stat[:, 7:8],
                    ucb[:, pp * D:(pp + 1) * D], ALU.mult, ALU.add),
                    deps=[t_om, t_rst, t_ul, t_x1_dma[pp]], give=True)
                t_x1_dma[pp] = self.dma("sync", sp_x1[r0:r0 + 128, :],
                                        x1b[:, pp * D:(pp + 1) * D], deps=[t_x1])
                t_ld_used[pp] = t_x1
            t_p2a_done = t_x1_dma[(NCHUNK - 1) % 2]
        es_pa.close()
        if not self.noalias:
            es_a.close()
        if not self.noalias:
            es_sp.close()

        es_f = contextlib.ExitStack()
        ln2_fm = self.sb(es_f, "ln2_fm", [128, KD * T2B], BF16)
        l2v = ln2_fm[:, :].rearrange("p (k t) -> p k t", k=KD)
        hs_all = self.sb(es_f, "hs_all", [128, NFS * T2B], BF16)
        hsv = hs_all[:, :].rearrange("p (s t) -> p s t", s=NFS)
        x1c = self.sb(es_f, "x1c", [128, D], F32)
        ln2tm = self.sb(es_f, "ln2tm", [128, D], BF16)
        rtb = self.sb(es_f, "rtb", [128, KD * 8], BF16)
        rtv = rtb[:, :].rearrange("p (k c) -> p k c", k=KD)
        prex = self.sb(es_f, "prex", [128, 8 * 8], F32)
        cw_t = self.sb(es_f, "cw_t", [128, 8 * 8], F32)
        cwT = self.sb(es_f, "cwT", [8, T2B], BF16)
        cwbc = self.sb(es_f, "cwbc", [128, T2B], BF16)
        wgb = self.sb(es_f, "wgb", [128, KD * 128], BF16)
        wub = self.sb(es_f, "wub", [128, KD * 128], BF16)
        sgb = self.sb(es_f, "sgb", [128, T2B], BF16)
        t1b = self.sb(es_f, "t1b", [128, T2B], BF16)
        wdb = self.sb(es_f, "wdb", [128, 2 * 1024], BF16)
        x1d = self.sb(es_f, "x1d", [128, 2 * 512], F32)
        x2b = self.sb(es_f, "x2b", [128, 2 * 512], F32)
        topb = self.sb(es_f, "topb", [128, 8], F32)
        rcpb = self.sb(es_f, "rcpb", [128, 8], F32)
        sel8b = self.sb(es_f, "sel8b", [8, 8 * 128], BF16)
        pd = [self.psum(es_f, f"pd{i}", [128, 1024], F32) for i in range(4)]

        if "p2b" in ph:
            nsub2 = T2B // 128
            t_sel = self.dma("gpsimd", sel8b[:, :], sel8_c, deps=[t_p2a_done])
            t_rt = self.dma("gpsimd", rtv[:, :, :],
                            router_w.rearrange("(k p) c -> p k c", p=128),
                            deps=[t_p2a_done])
            t_pdu = [None, None, None, None]
            t_down_done = None
            t_wdb_used = [None, None]
            t_wg_used = None
            t_x2dma = [None, None]
            t_x1d_used = [None, None]
            t_pe = None
            t_cwT_v = None
            t_cwb = None
            t_sg = None
            t_t1 = None
            t_last_hs = None
            t_xl_used = None
            for ti in range(N2B):
                tok0 = ti * T2B
                t_rmm = None
                t_trev = None
                for sj in range(nsub2):
                    r0 = tok0 + sj * 128
                    t_xl = self.dma("sync", x1c[:, :], sp_x1[r0:r0 + 128, :],
                                    deps=[t_p2a_done, t_xl_used])
                    t_bs = None
                    for j in range(4):
                        t_bs = self.emit("vector", lambda e, j=j: e.bn_stats(
                            bnst[:, 6 * j:6 * j + 6], x1c[:, 512 * j:512 * (j + 1)]),
                            deps=[t_xl] if j == 0 else (), give=(j == 3))
                    t_bn = self.emit("vector", lambda e: e.bn_aggr(
                        stat[:, 0:2], bnst[:, :]), deps=[t_bs], give=True,
                        keep_self=True)
                    self.emit("scalar", lambda e: e.activation(
                        stat[:, 2:3], stat[:, 1:2], AF.Ln, bias=one_col[:, 0:1],
                        scale=1e5), deps=[t_bn])
                    t_rs = self.emit("scalar", lambda e: e.activation(
                        stat[:, 3:4], stat[:, 2:3], AF.Exp, bias=hbl_col[:, 0:1],
                        scale=-0.5), give=True)
                    t_ln2 = self.emit("vector", lambda e: e.tensor_scalar(
                        ln2tm[:, :], x1c[:, :], stat[:, 0:1], stat[:, 3:4],
                        ALU.subtract, ALU.mult), deps=[t_rs, t_bn], give=True,
                        keep_self=True)
                    t_xl_used = t_ln2
                    pd0b = pd[0][:, :].bitcast(BF16)
                    t_tp = None
                    for k in range(KD):
                        t_tp = self.emit("tensor", lambda e, k=k, pd0b=pd0b:
                                         e.transpose(
                            pd0b[:, k * 128:(k + 1) * 128],
                            ln2tm[:, k * 128:(k + 1) * 128], identb[:, :]),
                            deps=[t_ln2, t_pdu[0], t_pdu[1]] if k == 0 else (),
                            give=(k == KD - 1))
                    t_trev = self.emit("vector", lambda e, sj=sj, pd0b=pd0b:
                                       e.tensor_copy(
                        l2v[:, :, sj * 128:(sj + 1) * 128],
                        pd0b[:, :].rearrange("p (k t) -> p k t", k=KD)),
                        deps=[t_tp], give=True)
                    t_pdu[0] = t_trev
                    t_pdu[1] = t_trev
                    for k in range(KD):
                        t_rmm = self.emit("tensor", lambda e, k=k, sj=sj: e.matmul(
                            pd[2][:, sj * 8:(sj + 1) * 8],
                            l2v[:, k, sj * 128:(sj + 1) * 128],
                            rtv[:, k, :], start=(k == 0), stop=(k == KD - 1)),
                            deps=[t_trev, t_rt, t_pdu[2]] if k == 0 else (),
                            give=(k == KD - 1))
                if "dbg_ln2" in self.dbg and ti == 0:
                    dbg_ln2 = self.nc.dram_tensor(
                        "dbg_ln2", [128, KD * T2B], BF16,
                        kind="ExternalOutput").ap()
                    self.dma("sync", dbg_ln2, ln2_fm[:, :], deps=[t_trev])
                t_pe = self.emit("scalar", lambda e: e.activation(
                    prex[:, :], pd[2][:, 0:64], AF.Exp), deps=[t_rmm], give=True)
                t_pdu[2] = t_pe
                t_red = self.emit("vector", lambda e: e.tensor_reduce(
                    rcpb[:, :], prex[:, :].rearrange("p (s c) -> p s c", s=8),
                    mybir.AxisListType.X, ALU.add), deps=[t_pe, t_cwT_v],
                    give=True)
                t_rcp = self.emit("vector", lambda e: e.reciprocal(
                    rcpb[:, :], rcpb[:, :]), deps=[t_red], give=True,
                    keep_self=True)
                rcv = rcpb[:, :].unsqueeze(2).to_broadcast([128, 8, 8])
                self.emit("vector", lambda e, rcv=rcv: e.tensor_tensor(
                    prex[:, :].rearrange("p (s c) -> p s c", s=8),
                    prex[:, :].rearrange("p (s c) -> p s c", s=8), rcv, ALU.mult),
                    deps=[t_rcp], keep_self=True)
                t_cw = None
                for sj in range(nsub2):
                    t_mx = self.emit("vector", lambda e, sj=sj: e.max(
                        out=topb[:, :], in_=prex[:, sj * 8:(sj + 1) * 8]),
                        give=True)
                    t_cw = self.emit("vector", lambda e, sj=sj: e.scalar_tensor_tensor(
                        cw_t[:, sj * 8:(sj + 1) * 8], prex[:, sj * 8:(sj + 1) * 8],
                        topb[:, 1:2], prex[:, sj * 8:(sj + 1) * 8],
                        ALU.is_ge, ALU.mult), deps=[t_mx], keep_self=True,
                        give=(sj == nsub2 - 1))
                t_ct = None
                for sj in range(nsub2):
                    t_ct = self.emit("tensor", lambda e, sj=sj: e.transpose(
                        pd[3][0:8, sj * 128:(sj + 1) * 128],
                        cw_t[:, sj * 8:(sj + 1) * 8], identf[:, :]),
                        deps=[t_cw, t_pdu[3]] if sj == 0 else (),
                        give=(sj == nsub2 - 1))
                t_cwT_v = self.emit("vector", lambda e: e.tensor_copy(
                    cwT[:, :], pd[3][0:8, 0:T2B]), deps=[t_ct], give=True)
                t_pdu[3] = t_cwT_v
                first_hs_of_tile = True
                for ei in range(10):
                    is_sh = ei >= 8
                    grow = (2 * ei, 2 * ei + 1) if not is_sh else \
                        (2 * E + (ei - 8), 2 * E + 2 + (ei - 8))
                    sbase = ei * 6
                    if not is_sh:
                        t_cm = None
                        for nn in range(T2B // 512):
                            t_cm = self.emit("tensor", lambda e, ei=ei, nn=nn: e.matmul(
                                pd[3][:, nn * 512:(nn + 1) * 512],
                                sel8b[:, ei * 128:(ei + 1) * 128],
                                cwT[:, nn * 512:(nn + 1) * 512], start=True, stop=True),
                                deps=[t_cwT_v, t_pdu[3], t_sel] if nn == 0 else (),
                                give=(nn == T2B // 512 - 1))
                        t_cwb = self.emit("vector", lambda e: e.tensor_copy(
                            cwbc[:, :], pd[3][:, 0:T2B]),
                            deps=[t_cm, t_last_hs], give=True)
                        t_pdu[3] = t_cwb
                    for st_i in range(sbase, sbase + 6):
                        fi = st_i - sbase
                        gsrc = sc_gu[grow[0], :, fi * 128:(fi + 1) * 128]
                        usrc = sc_gu[grow[1], :, fi * 128:(fi + 1) * 128]
                        t_gl = self.dma("sync",
                                        wgb[:, :].rearrange("p (k c) -> p k c", k=KD),
                                        gsrc.rearrange("(k p) c -> p k c", p=128),
                                        deps=[t_wg_used, t_cv_done, t_cv_done2])
                        t_ul2 = self.dma("sync",
                                         wub[:, :].rearrange("p (k c) -> p k c", k=KD),
                                         usrc.rearrange("(k p) c -> p k c", p=128),
                                         deps=[t_wg_used])
                        wgv = wgb[:, :].rearrange("p (k c) -> p k c", k=KD)
                        wuv = wub[:, :].rearrange("p (k c) -> p k c", k=KD)
                        t_gm = None
                        for k in range(KD):
                            f = (k == 0)
                            l = (k == KD - 1)
                            for nn in range(T2B // 512):
                                self.emit("tensor", lambda e, k=k, nn=nn, f=f, l=l,
                                          wgv=wgv: e.matmul(
                                    pd[0][:, nn * 512:(nn + 1) * 512],
                                    wgv[:, k, :],
                                    l2v[:, k, nn * 512:(nn + 1) * 512],
                                    start=f, stop=l),
                                    deps=[t_gl, t_ul2, t_down_done, t_sg,
                                          t_pdu[0]] if (f and nn == 0) else ())
                        for k in range(KD):
                            f = (k == 0)
                            l = (k == KD - 1)
                            for nn in range(T2B // 512):
                                t_gm = self.emit("tensor", lambda e, k=k, nn=nn,
                                                 f=f, l=l, wuv=wuv: e.matmul(
                                    pd[1][:, nn * 512:(nn + 1) * 512],
                                    wuv[:, k, :],
                                    l2v[:, k, nn * 512:(nn + 1) * 512],
                                    start=f, stop=l),
                                    deps=[t_t1, t_pdu[1]] if (f and nn == 0) else (),
                                    give=(l and nn == T2B // 512 - 1))
                        t_sg = self.emit("scalar", lambda e: e.activation(
                            sgb[:, :], pd[0][:, 0:T2B], AF.Silu),
                            deps=[t_gm, t_t1], give=True)
                        t_pdu[0] = t_sg
                        t_t1 = self.emit("vector", lambda e: e.tensor_tensor(
                            t1b[:, :], sgb[:, :], pd[1][:, 0:T2B], ALU.mult),
                            deps=[t_sg, t_last_hs], give=True)
                        t_pdu[1] = t_t1
                        dep_hs = [t_t1, t_cwb]
                        if first_hs_of_tile:
                            dep_hs.append(t_down_done)
                            first_hs_of_tile = False
                        if not is_sh:
                            t_last_hs = self.emit(
                                "gpsimd", lambda e, st_i=st_i: e.tensor_tensor(
                                    hsv[:, st_i, :], t1b[:, :], cwbc[:, :],
                                    ALU.mult), deps=dep_hs, give=True)
                        else:
                            t_last_hs = self.emit(
                                "gpsimd", lambda e, st_i=st_i: e.tensor_copy(
                                    hsv[:, st_i, :], t1b[:, :]),
                                deps=dep_hs, give=True)
                        t_wg_used = t_gm
                if "dbg_hs" in self.dbg and ti == 0:
                    dbg_hs = self.nc.dram_tensor(
                        "dbg_hs", [128, NFS * T2B], BF16,
                        kind="ExternalOutput").ap()
                    self.dma("sync", dbg_hs, hs_all[:, :], deps=[t_last_hs])
                    dbg_cw = self.nc.dram_tensor(
                        "dbg_cw", [128, 3 * T2B], BF16,
                        kind="ExternalOutput").ap()
                    self.dma("sync", dbg_cw[:, 0:T2B], cwbc[:, :], deps=[t_cwb])
                    self.dma("sync", dbg_cw[:, T2B:2 * T2B], sgb[:, :],
                             deps=[t_sg])
                    self.dma("sync", dbg_cw[:, 2 * T2B:], t1b[:, :], deps=[t_t1])
                def wd_row(st_i):
                    ei2 = st_i // 6
                    kk = st_i % 6
                    if ei2 < E:
                        return ei2 * FE + kk * 128
                    return E * FE + (ei2 - 8) * FE + kk * 128
                t_dm = None
                t_pdd = list(t_pdu)
                for q2 in range(2):
                    for dqh in range(2):
                        for st_i in range(NFS):
                            row = wd_row(st_i)
                            wp = st_i % 2
                            t_wdl = self.dma("sync", wdb[:, wp * 1024:(wp + 1) * 1024],
                                             sc_wd[row:row + 128,
                                                   dqh * 1024:(dqh + 1) * 1024],
                                             deps=[t_wdb_used[wp]])
                            for sq in range(4):
                                sj = q2 * 4 + sq
                                f = (st_i == 0)
                                l = (st_i == NFS - 1)
                                dd = ()
                                if f:
                                    dd = [t_wdl, t_last_hs, t_pe, t_pdd[sq]]
                                elif sq == 0:
                                    dd = [t_wdl]
                                for nn in range(2):
                                    t_dm = self.emit("tensor", lambda e, st_i=st_i,
                                                     sj=sj, sq=sq, nn=nn, wp=wp, f=f,
                                                     l=l: e.matmul(
                                        pd[sq][:, nn * 512:(nn + 1) * 512],
                                        hsv[:, st_i, sj * 128:(sj + 1) * 128],
                                        wdb[:, wp * 1024 + nn * 512:
                                            wp * 1024 + (nn + 1) * 512],
                                        start=f, stop=l),
                                        deps=dd if nn == 0 else (),
                                        give=(sq == 3 and nn == 1))
                            t_wdb_used[wp] = t_dm
                        for sq in range(4):
                            sj = q2 * 4 + sq
                            r0 = tok0 + sj * 128
                            xp = sq % 2
                            t_xd = self.dma("sync", x1d[:, xp * 512:(xp + 1) * 512],
                                            sp_x1[r0:r0 + 128,
                                                  dqh * 1024:dqh * 1024 + 512],
                                            deps=[t_x1d_used[xp]])
                            t_x2 = self.emit("vector", lambda e, sq=sq, xp=xp:
                                             e.tensor_tensor(
                                x2b[:, xp * 512:(xp + 1) * 512], pd[sq][:, 0:512],
                                x1d[:, xp * 512:(xp + 1) * 512], ALU.add),
                                deps=[t_dm, t_xd, t_x2dma[xp]], give=True)
                            t_x2dma[xp] = self.dma(
                                "sync", out[r0:r0 + 128, dqh * 1024:dqh * 1024 + 512],
                                x2b[:, xp * 512:(xp + 1) * 512], deps=[t_x2])
                            t_xd2 = self.dma("sync", x1d[:, xp * 512:(xp + 1) * 512],
                                             sp_x1[r0:r0 + 128, dqh * 1024 + 512:
                                                   dqh * 1024 + 1024],
                                             deps=[t_x2])
                            t_x22 = self.emit("vector", lambda e, sq=sq, xp=xp:
                                              e.tensor_tensor(
                                x2b[:, xp * 512:(xp + 1) * 512], pd[sq][:, 512:1024],
                                x1d[:, xp * 512:(xp + 1) * 512], ALU.add),
                                deps=[t_xd2, t_x2dma[xp]], give=True)
                            t_x2dma[xp] = self.dma(
                                "sync",
                                out[r0:r0 + 128, dqh * 1024 + 512:(dqh + 1) * 1024],
                                x2b[:, xp * 512:(xp + 1) * 512], deps=[t_x22])
                            t_x1d_used[xp] = t_x22
                            t_pdd[sq] = t_x22
                t_down_done = t_dm
                t_pdu = t_pdd
            self.wait("sync", [t_x2dma[0], t_x2dma[1]])
        if not self.noalias:
            es_f.close()
        for qk in ("sync_q", "scalar_q", "gpsimd_q"):
            if self.scnt[qk]:
                self.wait("sync", [Tok(qk, self.scnt[qk], "x", "dma")])
        return self.finalize()


def _host_prepare(inputs):
    u = np.asarray(inputs["L0_attn_u"], dtype=np.float32)
    w_in_t = np.ascontiguousarray(np.asarray(inputs["L0_attn_W_in"], np.float32).T)
    conv_w = np.asarray(inputs["L0_attn_conv_w"], np.float32)
    conv_w_t = np.ascontiguousarray(
        conv_w.reshape(NXS, 128, 4).transpose(1, 0, 2).reshape(128, NXS * 4))
    conv_b_t = np.ascontiguousarray(
        np.asarray(inputs["L0_attn_conv_b"], np.float32).reshape(NXS, 128).T)
    zb = np.ascontiguousarray(np.broadcast_to(
        np.asarray(inputs["L0_attn_z_bias"], np.float32), (128, D)))
    dbc = np.ascontiguousarray(np.broadcast_to(
        np.asarray(inputs["L0_attn_D"], np.float32), (128, NH)))
    rmsw_t = np.ascontiguousarray(
        np.asarray(inputs["L0_attn_rms_w"], np.float32).reshape(KD, 128).T)
    w_out_t = np.ascontiguousarray(np.asarray(inputs["L0_attn_W_out"], np.float32).T)
    ident = np.eye(128, dtype=np.float32)
    sl = np.arange(128)
    triu = (sl[:, None] <= sl[None, :]).astype(np.float32)
    blk = ((sl[:, None] // 64 == sl[None, :] // 64) &
           (sl[:, None] <= sl[None, :])).astype(np.float32)
    ones = np.ones((128, 128), np.float32)
    h0 = np.zeros((128, 128), np.float32)
    h0[:64, :] = 1.0
    sel8 = np.zeros((8, 8 * 128), np.float32)
    for e_ in range(8):
        sel8[e_, e_ * 128:(e_ + 1) * 128] = 1.0
    common = dict(
        w_in_t=w_in_t, conv_w_t=conv_w_t, conv_b_t=conv_b_t, zb_bc=zb, d_bc=dbc,
        rmsw_t=rmsw_t, w_out_t=w_out_t,
        router_w=np.asarray(inputs["L0_ffn_router"], np.float32),
        sh_gate=np.asarray(inputs["L0_ffn_shared_gate"], np.float32),
        sh_up=np.asarray(inputs["L0_ffn_shared_up"], np.float32),
        sh_down=np.asarray(inputs["L0_ffn_shared_down"], np.float32),
        wg=np.asarray(inputs["L0_ffn_w_gate"], np.float32),
        wu=np.asarray(inputs["L0_ffn_w_up"], np.float32),
        wd=np.asarray(inputs["L0_ffn_w_down"], np.float32),
        ident_c=ident, triu_c=triu, blktriu_c=blk, ones_c=ones, h0ones_c=h0,
        sel8_c=sel8,
    )
    maps = []
    for core in range(8):
        b, q = core // 4, core % 4
        m = dict(common)
        m["u_main"] = np.ascontiguousarray(u[b, q * TC:(q + 1) * TC, :])
        m["u_halo"] = (np.zeros((128, D), np.float32) if q == 0
                       else np.ascontiguousarray(u[b, q * TC - 128:q * TC, :]))
        posm = np.zeros((128, 4), np.float32)
        posm[:, q] = 1.0
        m["possel"] = posm
        maps.append(m)
    return maps


def kernel(**inputs):
    maps = _host_prepare(inputs)
    kb = KB()
    nc = kb.build()
    res = run_bass_kernel_spmd(nc, maps, core_ids=list(range(8)))
    outs = [res.results[c]["out"] for c in range(8)]
    full = np.stack([np.concatenate(outs[0:4], 0), np.concatenate(outs[4:8], 0)], 0)
    return full.astype(np.float32)
